# version 04 (final)
# baseline (speedup 1.0000x reference)
import contextlib
import numpy as np
import ml_dtypes as md

import concourse.bass as bass
import concourse.mybir as mybir
from concourse.bass_utils import run_bass_kernel_spmd

F32 = mybir.dt.float32
BF16 = mybir.dt.bfloat16
AF = mybir.ActivationFunctionType
ALU = mybir.AluOpType
AX = mybir.AxisListType

NDC = 16
NEC = 64
NG = 4
HPG = 4
INVSQ = 0.08838834764831845
PI = 3.14159265358979323846


class Plan:
    def __init__(self):
        self.streams = {e: [] for e in "tsvgy"}
        self.cnt = dict(T=0, S=0, V=0, dg=0, dy=0, da=0)

    def emit(self, eng, fn, waits=(), incs=()):
        for s, n in incs:
            self.cnt[s] += n
        self.streams[eng].append(
            (fn, [w for w in waits if w is not None and w[1]], list(incs)))
        return self.cnt[incs[0][0]] if incs else None


def build_nc():
    nc = bass.Bass("TRN2", target_bir_lowering=False)

    x_h = nc.dram_tensor("x", [2048, 2048], F32, kind="ExternalInput").ap()
    wqkv_h = nc.dram_tensor("wqkv", [48, 128, 2048], F32, kind="ExternalInput").ap()
    wo_h = nc.dram_tensor("wo", [16, 128, 2048], F32, kind="ExternalInput").ap()
    wu_h = nc.dram_tensor("wu", [64, 128, 2048], F32, kind="ExternalInput").ap()
    wg_h = nc.dram_tensor("wg", [64, 128, 2048], F32, kind="ExternalInput").ap()
    wd_h = nc.dram_tensor("wd", [16, 128, 8192], F32, kind="ExternalInput").ap()
    bq_h = nc.dram_tensor("bq", [128, 16], F32, kind="ExternalInput").ap()
    bk_h = nc.dram_tensor("bk", [128, 16], F32, kind="ExternalInput").ap()
    bv_h = nc.dram_tensor("bv", [128, 16], F32, kind="ExternalInput").ap()
    bo_h = nc.dram_tensor("bo", [128, 16], F32, kind="ExternalInput").ap()
    rot_h = nc.dram_tensor("rot", [128, 16], F32, kind="ExternalInput").ap()
    spl_h = nc.dram_tensor("spl", [128, 16], F32, kind="ExternalInput").ap()
    id_h = nc.dram_tensor("identh", [128, 128], BF16, kind="ExternalInput").ap()
    out_h = nc.dram_tensor("out", [2048, 1024], F32, kind="ExternalOutput").ap()

    es = contextlib.ExitStack()
    sb = lambda n, sh, dt: es.enter_context(nc.sbuf_tensor(n, sh, dt))
    psal = lambda n, sh, dt: es.enter_context(nc.psum_tensor(n, sh, dt))

    big = sb("big", [128, 32768], BF16)
    kbuf = sb("kbuf", [128, 8192], BF16)
    vbuf = sb("vbuf", [128, 8192], BF16)
    qbuf = sb("qbuf", [128, 4096], BF16)
    aog = sb("aog", [128, 4096], BF16)
    x1 = sb("x1", [128, 16384], F32)
    wbuf = sb("wbuf", [128, 4096], BF16)
    fbuf = sb("fbuf", [128, 2048], F32)
    qtmp = sb("qtmp", [128, 1024], BF16)
    expt = sb("expt", [128, 1024], BF16)
    lnbc = sb("lnbc", [128, 1024], BF16)
    rbc = lnbc
    st0 = sb("st0", [1, 2048], F32)
    scl = sb("scl", [1, 16], F32)
    thr = sb("thr", [128, 8], F32)
    thp = sb("thp", [128, 3], F32)
    strip = sb("strip", [128, 56], F32)
    strip2 = sb("strip2", [128, 4], F32)
    bq_t = sb("bq_t", [128, 16], F32)
    bk_t = sb("bk_t", [128, 16], F32)
    bv_t = sb("bv_t", [128, 16], F32)
    bo_t = sb("bo_t", [128, 16], F32)
    rot_t = sb("rot_t", [128, 16], F32)
    spl_t = sb("spl_t", [128, 16], F32)
    qscale = sb("qscale", [128, 16], F32)
    qbias = sb("qbias", [128, 16], F32)
    kscale = sb("kscale", [128, 16], F32)
    kbias = sb("kbias", [128, 16], F32)
    t16a = sb("t16a", [128, 16], F32)
    t16b = sb("t16b", [128, 16], F32)
    t16c = sb("t16c", [128, 16], F32)
    t16d = sb("t16d", [128, 16], F32)
    ones_cb = sb("ones_cb", [128, 1], BF16)
    ones_cf = sb("ones_cf", [128, 1], F32)
    ones_rf = sb("ones_rf", [1, 128], F32)
    cvec = sb("cvec", [128, 8], F32)
    c_negpi = cvec[:, 0:1]
    c_pi = cvec[:, 1:2]
    c_hpi = cvec[:, 2:3]
    c_eps = cvec[:, 3:4]
    c_i2048 = cvec[:, 4:5]
    c_ni2048 = cvec[:, 5:6]
    gapc = cvec[:, 6:7]

    qdb = wbuf
    ident = qtmp[:, 0:128]
    ones_rb = qtmp[0:1, 128:256]
    avtmp = qtmp[:, 256:768]
    hs_slot = lambda i: (expt[:, 0:1024] if i % 2 == 0 else lnbc[:, 0:1024])
    vstage = lambda i: expt[:, (i % 2) * 512:(i % 2) * 512 + 512]
    PS0 = psal("PS0", [128, 1024], F32)
    PS1 = psal("PS1", [128, 1024], F32)
    PS2 = psal("PS2", [128, 1024], F32)
    PS3 = psal("PS3", [128, 512], F32)
    PT = psal("PT", [128, 512], BF16)

    sems = {}
    for sname in ("T", "S", "V", "dg", "dy", "da"):
        sems[sname] = es.enter_context(nc.semaphore("sem_" + sname))

    p = Plan()

    def mm(psum_ap, lhsT, rhs, start, stop, waits=(), inc=False):
        return p.emit(
            "t",
            lambda e, a=psum_ap, b=lhsT, c=rhs, s1=start, s2=stop: e.matmul(
                a, b, c, start=s1, stop=s2, skip_group_check=True),
            waits, [("T", 1)] if inc else ())

    def tr(psum_ap, in_ap, waits=(), inc=False):
        return p.emit(
            "t", lambda e, a=psum_ap, b=in_ap: e.transpose(a, b, ident),
            waits, [("T", 1)] if inc else ())

    def act(out, in_, func, bias=0.0, scale=1.0, waits=()):
        return p.emit(
            "s",
            lambda e, o=out, i=in_, f=func, b=bias, sc=scale: e.activation(
                o, i, f, bias=b, scale=sc),
            waits, [("S", 1)])

    def vop(fn, waits=(), inc=True):
        return p.emit("v", fn, waits, [("V", 1)] if inc else ())

    def _recip(e, o, i):
        with nc.allow_low_precision(reason="bf16 recip (fp32 internal, budgeted)"):
            return e.reciprocal(o, i)

    def dma(eng, sem, out, in_, waits=()):
        w = list(waits)
        if p.cnt[sem] > 0:
            w.append((sem, p.cnt[sem]))
        return p.emit(
            eng, lambda e, o=out, i=in_: e.dma_start(out=o, in_=i),
            w, [(sem, 16)])

    lnx = lambda dc: big[:, dc * 2048:(dc + 1) * 2048]
    hid = lambda j: big[:, j * 1024:(j + 1) * 1024]
    kview = lambda l: kbuf[:, l * 2048:(l + 1) * 2048]
    vtile = lambda kc, l: vbuf[:, kc * 512 + l * 128: kc * 512 + (l + 1) * 128]
    qview = lambda l: qbuf[:, l * 1024:(l + 1) * 1024]
    aview = lambda l: aog[:, l * 1024:(l + 1) * 1024]
    x1v = lambda dc: x1[:, dc * 1024:(dc + 1) * 1024]
    wslot = lambda i: wbuf[:, (i % 2) * 2048:((i % 2) + 1) * 2048]
    fslot = lambda i: fbuf[:, (i % 2) * 1024:((i % 2) + 1) * 1024]
    lnx1 = lambda dc: (kbuf[:, dc * 1024:(dc + 1) * 1024] if dc < 8
                       else vbuf[:, (dc - 8) * 1024:(dc - 7) * 1024])

    for tile, val in (
        (ones_cb[:], 1.0), (ones_rb, 1.0), (ones_cf[:], 1.0), (ones_rf[:], 1.0),
        (c_negpi, -PI), (c_pi, PI), (c_hpi, PI / 2), (c_eps, 1e-5),
        (c_i2048, 1.0 / 2048.0), (c_ni2048, -1.0 / 2048.0),
    ):
        vop(lambda e, a=tile, v=val: e.memset(a, v), inc=False)
    v_consts = vop(lambda e: e.engine_nop())

    dy_small = None
    for dst, src in ((bq_t[:], bq_h), (bk_t[:], bk_h), (bv_t[:], bv_h),
                     (bo_t[:], bo_h), (rot_t[:], rot_h), (spl_t[:], spl_h),
                     (ident, id_h)):
        dy_small = dma("y", "dy", dst, src)
    dy_xmy = {}
    for hc in range(2):
        dy_xmy[hc] = dma(
            "y", "dy",
            x1[:, hc * 8192:(hc + 1) * 8192].rearrange("p (c t) -> p c t", c=8),
            x_h[hc * 1024:(hc + 1) * 1024, 0:1024].rearrange("(c p) t -> p c t", p=128))
    dg_x = {}
    for ci in range(4):
        dg_x[ci] = dma(
            "g", "dg",
            big[:, ci * 8192:(ci + 1) * 8192].rearrange("p (c s) -> p c s", c=4),
            x_h[ci * 512:(ci + 1) * 512, :].rearrange("(c p) s -> p c s", p=128))

    slab_meta = []
    for g in range(NG):
        for j in range(12):
            slab_meta.append(("qkv", g * 12 + j))
        for dc in range(NDC):
            slab_meta.append(("wo", dc))
    slab_dy = [None] * len(slab_meta)
    slab_consumed = [None] * len(slab_meta)
    next_slab = [0]

    def emit_slab_dma(n):
        kind, j = slab_meta[n]
        src = wqkv_h[j] if kind == "qkv" else wo_h[j]
        waits = []
        if n >= 2:
            waits.append(("T", slab_consumed[n - 2]))
        slab_dy[n] = dma("g", "dg", wslot(n), src, waits=waits)

    emit_slab_dma(0)
    emit_slab_dma(1)
    next_slab[0] = 2

    def need_slab(n):
        while next_slab[0] <= n + 1 and next_slab[0] < len(slab_meta):
            emit_slab_dma(next_slab[0])
            next_slab[0] += 1
        return ("dg", slab_dy[n])

    s_x = act(t16a[:], rot_t[:], AF.Sign, bias=c_negpi,
              waits=[("dy", dy_small), ("V", v_consts)])
    s_sg2 = act(t16b[:], rot_t[:], AF.Sign, bias=c_pi)
    v_tt = vop(lambda e: e.tensor_tensor(t16a[:], t16a[:], t16b[:], ALU.add),
               waits=[("S", s_sg2)])
    v_negr = vop(lambda e: e.scalar_tensor_tensor(
        t16b[:], t16a[:], c_pi, rot_t[:], ALU.mult, ALU.subtract),
        waits=[("V", v_tt)])
    s_abs = act(t16c[:], t16b[:], AF.Abs, waits=[("V", v_negr)])
    s_sin = act(t16a[:], t16b[:], AF.Sin)
    s_cos = act(t16d[:], t16c[:], AF.Sin, bias=c_hpi, scale=-1.0,
                waits=[("S", s_abs)])
    v_q1 = vop(lambda e: e.scalar_tensor_tensor(
        qscale[:], t16d[:], INVSQ, spl_t[:], ALU.mult, ALU.mult),
        waits=[("S", s_cos)])
    v_ks = vop(lambda e: e.tensor_scalar(kscale[:], t16a[:], -1.0, None,
                                         ALU.mult), waits=[("S", s_sin)])
    vop(lambda e: e.tensor_tensor(qbias[:], bq_t[:], qscale[:], ALU.mult),
        waits=[("V", v_q1)], inc=False)
    v_scales = vop(lambda e: e.tensor_tensor(kbias[:], bk_t[:], kscale[:],
                                             ALU.mult), waits=[("V", v_ks)])

    s_xmy = {}
    for dc in range(NDC):
        s_xmy[dc] = act(x1v(dc), x1v(dc), AF.Identity, bias=bo_t[:, dc:dc + 1],
                        waits=[("dy", dy_xmy[dc // 8]), ("dy", dy_small)])

    def emit_layernorm(src_view, n_strips, tag, first_mm_waits=(),
                       cast_done=None, src_dg=None):
        inv_n = 1.0 / 2048.0
        nsh = n_strips // 2
        v_m = v_e2 = v_negmr = None
        v_apply = {}
        prev_last = None
        tref = {}
        for sh in range(nsh):
            for dc in range(NDC):
                fi = sh * NDC + dc
                wbase = []
                if src_dg is not None:
                    wbase.append(("dg", src_dg[dc // 4]))
                if cast_done is not None:
                    wbase.append(("S", cast_done[dc]))
                for q in range(2):
                    fq = fi * 2 + q
                    slot = expt[:, (fq % 2) * 512:((fq % 2) + 1) * 512]
                    wsq = list(wbase)
                    if fq >= 2:
                        wsq.append(("T", tref[fq - 2]))
                    s_sq = act(slot,
                               src_view(dc)[:, sh * 1024 + q * 512:
                                            sh * 1024 + (q + 1) * 512],
                               AF.Square, waits=wsq)
                    w = list(wbase)
                    if dc == 0 and q == 0:
                        if sh == 0:
                            w += list(first_mm_waits)
                        if v_m is not None:
                            w.append(("V", v_m))
                    mm(PS0[0:1, q * 512:(q + 1) * 512], ones_cb[:],
                       src_view(dc)[:, sh * 1024 + q * 512:sh * 1024 + (q + 1) * 512],
                       start=(dc == 0), stop=(dc == NDC - 1), waits=w)
                    w = [("S", s_sq)]
                    if dc == 0 and q == 0:
                        if sh == 0:
                            w += list(first_mm_waits)
                        if v_e2 is not None:
                            w.append(("V", v_e2))
                    tc = mm(PS1[0:1, q * 512:(q + 1) * 512], ones_cb[:],
                            slot[:, :],
                            start=(dc == 0), stop=(dc == NDC - 1),
                            waits=w, inc=True)
                    tref[fq] = tc
            t_done = tref[(sh * NDC + NDC - 1) * 2 + 1]
            RS = st0[0:1, 0:1024]
            NM = st0[0:1, 1024:2048]
            s_m2 = act(RS, PS0[0:1, :], AF.Square, scale=c_i2048[0:1, :],
                       waits=[("T", t_done)])
            v_var = vop(lambda e, RS=RS: e.scalar_tensor_tensor(
                RS, PS1[0:1, :], c_i2048[0:1, :], RS, ALU.mult, ALU.subtract),
                waits=[("S", s_m2)])
            s_std = act(RS, RS, AF.Sqrt, bias=c_eps[0:1, :],
                        waits=[("V", v_var)])
            RSa = st0[0:1, 0:512]
            RSb = st0[0:1, 512:1024]
            NMa = st0[0:1, 1024:1536]
            NMb = st0[0:1, 1536:2048]
            v_r1 = vop(lambda e, NMa=NMa, RSa=RSa: _recip(e, NMa, RSa),
                       waits=[("S", s_std)])
            v_r2 = vop(lambda e, NMb=NMb, RSb=RSb: _recip(e, NMb, RSb))
            vop(lambda e, RSa=RSa, NMa=NMa: e.scalar_tensor_tensor(
                RSa, PS0[0:1, 0:512], c_ni2048[0:1, :], NMa,
                ALU.mult, ALU.mult), waits=[("V", v_r1)], inc=False)
            v_negmr = vop(lambda e, RSb=RSb, NMb=NMb: e.scalar_tensor_tensor(
                RSb, PS0[0:1, 512:1024], c_ni2048[0:1, :], NMb,
                ALU.mult, ALU.mult), waits=[("V", v_r2)])
            v_e2 = v_negmr
            for spq in range(2):
                sp = sh * 2 + spq
                w1 = [("V", v_negmr)]
                if prev_last is not None:
                    w1.append(("V", prev_last))
                t_b1 = mm(PS0[:, 0:512], ones_rf[:],
                          st0[0:1, 1024 + spq * 512:1024 + (spq + 1) * 512],
                          True, True, waits=w1, inc=True)
                t_b2 = mm(PS0[:, 512:1024], ones_rf[:],
                          st0[0:1, spq * 512:(spq + 1) * 512],
                          True, True, inc=True)

                def _mul(e, dc, sp):
                    return e.tensor_tensor(
                        src_view(dc)[:, sp * 512:(sp + 1) * 512],
                        src_view(dc)[:, sp * 512:(sp + 1) * 512],
                        PS0[:, 0:512], ALU.mult)

                def _add(e, dc, sp):
                    return e.tensor_tensor(
                        src_view(dc)[:, sp * 512:(sp + 1) * 512],
                        src_view(dc)[:, sp * 512:(sp + 1) * 512],
                        PS0[:, 512:1024], ALU.add)

                v_mul = {}
                for dc in range(NDC):
                    v_mul[dc] = vop(lambda e, dc=dc, sp=sp: _mul(e, dc, sp),
                                    waits=[("T", t_b1)])
                    if dc > 0:
                        v_apply[dc - 1] = vop(
                            lambda e, dc=dc - 1, sp=sp: _add(e, dc, sp),
                            waits=[("T", t_b2), ("V", v_mul[dc - 1])])
                v_apply[NDC - 1] = vop(
                    lambda e, dc=NDC - 1, sp=sp: _add(e, dc, sp),
                    waits=[("V", v_mul[NDC - 1])])
                prev_last = v_apply[NDC - 1]
            v_m = prev_last
        return v_apply, prev_last

    v_ln1, v_bc1 = emit_layernorm(lnx, 4, "ln1",
                                  first_mm_waits=[("V", v_consts),
                                                  ("dy", dy_small)],
                                  src_dg=dg_x)
    v_ln1_all = max(v_ln1.values())
    ps0_free = ("V", v_bc1)
    ps1_free = ("V", max(v_ln1.values()))

    p1_meta = []
    for mi, wh in ((0, wu_h), (1, wg_h)):
        for sl in range(64):
            for hf in range(2):
                p1_meta.append((wh, sl, hf, 0))
    for sl in range(16):
        for hf in range(8):
            p1_meta.append((wd_h, sl, hf, 0))
    p1_dy = [None] * len(p1_meta)
    p1_vdone = [None] * len(p1_meta)
    p1_ndma = [0]
    p1_nred = [0]
    p1_lvl2 = [None]

    def p1_pump(k):
        while p1_ndma[0] <= k and p1_ndma[0] < len(p1_meta):
            n = p1_ndma[0]
            wh, sl, hf, col = p1_meta[n]
            waits = []
            if n >= 2:
                waits.append(("V", p1_vdone[n - 2]))
            p1_dy[n] = dma("y", "dy", fslot(n),
                           wh[sl][:, hf * 1024:(hf + 1) * 1024], waits=waits)
            p1_ndma[0] += 1

    def p1_reduce(count):
        for _ in range(count):
            n = p1_nred[0]
            if n >= len(p1_meta):
                return
            p1_pump(n)
            wl1 = [("dy", p1_dy[n])]
            if n % 8 == 0 and p1_lvl2[0] is not None:
                wl1.append(("V", p1_lvl2[0]))
            v_l1 = vop(lambda e, n=n: e.tensor_reduce(
                strip[:, n % 8:n % 8 + 1], fslot(n), AX.X, ALU.add,
                apply_absolute_value=True), waits=wl1)
            p1_vdone[n] = v_l1
            if n % 8 == 7:
                p1_vdone[n] = vop(lambda e, n=n: e.tensor_reduce(
                    strip[:, 8 + n // 8:9 + n // 8], strip[:, 0:8],
                    AX.X, ALU.add), waits=[("V", v_l1)])
                p1_lvl2[0] = p1_vdone[n]
            p1_nred[0] += 1
            p1_pump(n + 2)

    s_evac = {}
    v_x1 = {dc: None for dc in range(NDC)}
    vst_free = {0: None, 1: None}
    pt_free = None
    sums_free = None
    ps3_free = None
    ps2_free = None
    prev_rbc = None
    prev_ao = None
    prev_exp_done = None
    v_rc_last = None
    t_attn_last = None
    t_d_last = None
    for g in range(NG):
        for l in range(HPG):
            fc = g * HPG + l
            n = g * 28 + l
            wslab = wslot(n)
            for th in range(2):
                for dc in range(NDC):
                    w = []
                    if th == 0 and dc == 0:
                        w = [need_slab(n), ("V", v_ln1_all)]
                        if ps2_free is not None:
                            w.append(ps2_free)
                    tq = mm(PS2[:, th * 512:(th + 1) * 512],
                            wslab[:, dc * 128:(dc + 1) * 128],
                            lnx(dc)[:, th * 512:(th + 1) * 512],
                            start=(dc == 0), stop=(dc == NDC - 1), waits=w,
                            inc=(th == 1 and dc == NDC - 1))
            slab_consumed[n] = tq
            s_evac[("q", g, l)] = act(
                qview(l), PS2[:, :], AF.Identity,
                bias=qbias[:, fc:fc + 1], scale=qscale[:, fc:fc + 1],
                waits=[("T", tq), ("V", v_scales)])
            ps2_free = ("S", s_evac[("q", g, l)])
        for l in range(HPG):
            fc = g * HPG + l
            n = g * 28 + 4 + l
            wslab = wslot(n)
            for sh in range(2):
                pst = PS0 if sh == 0 else PS1
                pfree = ps0_free if sh == 0 else ps1_free
                for th in range(2):
                    for dc in range(NDC):
                        w = []
                        if th == 0 and dc == 0:
                            w = [need_slab(n)]
                            if pfree is not None:
                                w.append(pfree)
                        tk = mm(pst[:, th * 512:(th + 1) * 512],
                                wslab[:, dc * 128:(dc + 1) * 128],
                                lnx(dc)[:, sh * 1024 + th * 512:
                                        sh * 1024 + (th + 1) * 512],
                                start=(dc == 0), stop=(dc == NDC - 1), waits=w,
                                inc=(th == 1 and dc == NDC - 1))
                slab_consumed[n] = tk
                s_evac[("k", g, l, sh)] = act(
                    kview(l)[:, sh * 1024:(sh + 1) * 1024], pst[:, :],
                    AF.Identity, bias=kbias[:, fc:fc + 1],
                    scale=kscale[:, fc:fc + 1], waits=[("T", tk)])
                if sh == 0:
                    ps0_free = ("S", s_evac[("k", g, l, sh)])
                else:
                    ps1_free = ("S", s_evac[("k", g, l, sh)])
        for l in range(HPG):
            fc = g * HPG + l
            n = g * 28 + 8 + l
            wslab = wslot(n)
            for sh4 in range(4):
                for dc in range(NDC):
                    w = []
                    if dc == 0:
                        w = [need_slab(n)]
                        if ps3_free is not None:
                            w.append(ps3_free)
                    tv = mm(PS3[:, 0:512], wslab[:, dc * 128:(dc + 1) * 128],
                            lnx(dc)[:, sh4 * 512:(sh4 + 1) * 512],
                            start=(dc == 0), stop=(dc == NDC - 1), waits=w,
                            inc=(dc == NDC - 1))
                slab_consumed[n] = tv
                vslot = sh4 % 2
                vst = vstage(vslot)
                wv = [("T", tv)]
                if vst_free[vslot] is not None:
                    wv.append(("T", vst_free[vslot]))
                sev = act(vst, PS3[:, 0:512], AF.Identity,
                          bias=bv_t[:, fc:fc + 1], waits=wv)
                ps3_free = ("S", sev)
                wtr = [("S", sev)]
                if pt_free is not None:
                    wtr.append(("V", pt_free))
                ttr = None
                for j in range(4):
                    ttr = tr(PT[:, j * 128:(j + 1) * 128],
                             vst[:, j * 128:(j + 1) * 128],
                             waits=wtr if j == 0 else (), inc=(j == 3))
                vst_free[vslot] = ttr
                vv = None
                for j in range(4):
                    kc = sh4 * 4 + j
                    vv = vop(lambda e, kc=kc, l=l, j=j: e.tensor_copy(
                        vtile(kc, l), PT[:, j * 128:(j + 1) * 128]),
                        waits=[("T", ttr)] if j == 0 else ())
                pt_free = vv
        v_vdone = vv

        p1_reduce(60)

        sums_free = ps3_free
        k_last_evac = s_evac[("k", g, HPG - 1, 1)]
        for l in range(HPG):
            sq = s_evac[("q", g, l)]
            for th in range(2):
                exp_S = {}
                cons_T = {}
                for kc in range(16):
                    reg = kc % 4
                    pst = PS0 if reg < 2 else PS1
                    creg = pst[:, (reg % 2) * 512:((reg % 2) + 1) * 512]
                    w = []
                    if kc == 0:
                        w = [("S", sq), ("S", k_last_evac), ("V", v_vdone)]
                        if prev_exp_done is not None:
                            w.append(("S", prev_exp_done))
                        if prev_rbc is not None:
                            w.append(("V", prev_rbc))
                    elif kc >= 4:
                        w = [("S", exp_S[kc - 4])]
                    st = mm(creg,
                            kbuf[:, l * 2048 + kc * 128:l * 2048 + (kc + 1) * 128],
                            qview(l)[:, th * 512:(th + 1) * 512], True, True,
                            waits=w, inc=True)
                    eslot = expt[:, (kc % 2) * 512:((kc % 2) + 1) * 512]
                    we = [("T", st)]
                    if kc >= 2:
                        we.append(("T", cons_T[kc - 2]))
                    exp_S[kc] = act(eslot, creg, AF.Exp, waits=we)
                    ws = [("S", exp_S[kc])]
                    if kc == 0 and sums_free is not None:
                        ws.append(sums_free)
                    mm(PS3[0:1, 0:512], ones_cb[:], eslot,
                       start=(kc == 0), stop=(kc == 15), waits=ws)
                    wa = []
                    if kc == 0:
                        if l == 0 and th == 0:
                            wa = [ps2_free]
                        else:
                            wa = [ps2_free]
                    cons_T[kc] = mm(PS2[:, th * 512:(th + 1) * 512],
                                    vtile(kc, l), eslot,
                                    start=(kc == 0), stop=(kc == 15),
                                    waits=wa, inc=True)
                prev_exp_done = exp_S[15]
                v_rc = vop(lambda e: _recip(
                    e, st0[0:1, 0:512], PS3[0:1, 0:512]),
                    waits=[("T", cons_T[15])])
                sums_free = ("V", v_rc)
                v_rc_last = v_rc
                t_bc = mm(PS0[:, 0:512], ones_rf[:],
                          st0[0:1, 0:512], True, True,
                          waits=[("S", exp_S[12]), ("V", v_rc)], inc=True)
                wao = [("T", cons_T[15])]
                if prev_ao is not None:
                    wao.append(("V", prev_ao))
                s_ao = act(avtmp, PS2[:, th * 512:(th + 1) * 512],
                           AF.Identity, waits=wao)
                v_ao = vop(lambda e, l=l, th=th: e.tensor_tensor(
                    aview(l)[:, th * 512:(th + 1) * 512],
                    avtmp, PS0[:, 0:512], ALU.mult),
                    waits=[("S", s_ao), ("T", t_bc)])
                prev_rbc = v_ao
                prev_ao = v_ao
                ps2_free = ("S", s_ao)
                s_evac[("ao", g, l, th)] = v_ao
        t_attn_last = cons_T[15]
        v_aog_all = max(s_evac[("ao", g, l, th)]
                        for l in range(HPG) for th in range(2))
        ps0_free = ("V", prev_rbc)
        ps1_free = ("S", prev_exp_done)
        ps3_free = sums_free

        p1_reduce(38)

        for dc in range(NDC):
            n = g * 28 + 12 + dc
            wslab = wslot(n)
            pst = PS0 if dc % 2 == 0 else PS1
            pfree = ps0_free if dc % 2 == 0 else ps1_free
            for l in range(HPG):
                fc = g * HPG + l
                for th in range(2):
                    w = []
                    if l == 0 and th == 0:
                        w = [need_slab(n), ("V", v_aog_all)]
                        if pfree is not None:
                            w.append(pfree)
                    td = mm(pst[:, th * 512:(th + 1) * 512],
                            wslab[:, fc * 128:(fc + 1) * 128],
                            aview(l)[:, th * 512:(th + 1) * 512],
                            start=(l == 0), stop=(l == HPG - 1), waits=w,
                            inc=(l == HPG - 1 and th == 1))
            slab_consumed[n] = td
            wv = [("T", td), ("S", s_xmy[dc])]
            if v_x1[dc] is not None:
                wv.append(("V", v_x1[dc]))
            v_x1[dc] = vop(lambda e, dc=dc, pst=pst: e.tensor_tensor(
                x1v(dc), pst[:, :], x1v(dc), ALU.add), waits=wv)
            if dc % 2 == 0:
                ps0_free = ("V", v_x1[dc])
            else:
                ps1_free = ("V", v_x1[dc])
        t_d_last = slab_consumed[g * 28 + 12 + NDC - 1]

    s_cast1 = {}
    for dc in range(NDC):
        s_cast1[dc] = act(lnx1(dc), x1v(dc), AF.Identity,
                          waits=[("V", v_x1[dc])])
    v_ln2, v_bc2 = emit_layernorm(lnx1, 2, "ln2",
                                  first_mm_waits=[ps0_free, ps1_free],
                                  cast_done=s_cast1)
    v_ln2_all = max(v_ln2.values())
    ps0_free = ("V", v_bc2)
    ps1_free = ("V", v_ln2_all)

    p1_reduce(len(p1_meta) - p1_nred[0])
    p1_last_v = p1_vdone[len(p1_meta) - 1]

    v_s2 = None
    for mi in range(3):
        v_s2 = vop(lambda e, mi=mi: e.tensor_reduce(
            strip2[:, mi:mi + 1], strip[:, 8 + mi * 16:8 + (mi + 1) * 16],
            AX.X, ALU.add))
    w_sc = [("V", v_s2)]
    if ps3_free is not None:
        w_sc.append(ps3_free)
    t_sc = mm(PS3[0:1, 0:3], ones_cf[:], strip2[:, 0:3], True, True,
              waits=w_sc, inc=True)
    v_scl = vop(lambda e: e.tensor_scalar(
        scl[0:1, 0:3], PS3[0:1, 0:3], 1.0 / 16777216.0, 1e-8,
        ALU.mult, ALU.add), waits=[("T", t_sc)])
    t_bcs = mm(PS3[:, 16:19], ones_rf[:], scl[0:1, 0:3], True, True,
               waits=[("V", v_scl)], inc=True)
    v_tc = vop(lambda e: e.tensor_copy(thr[:, 0:3], PS3[:, 16:19]),
               waits=[("T", t_bcs)])
    vop(lambda e: e.tensor_scalar(thr[:, 3:6], PS3[:, 16:19], -0.5, None,
                                  ALU.mult), inc=False)
    vop(lambda e: e.tensor_scalar(thp[:, 0:3], PS3[:, 16:19], 0.5, None,
                                  ALU.mult), inc=False)
    vop(lambda e: e.tensor_scalar(thr[:, 6:7], PS3[:, 16:17], 0.5, None,
                                  ALU.mult), inc=False)
    v_thr = vop(lambda e: e.scalar_tensor_tensor(
        thr[:, 7:8], thr[:, 1:2], 0.25, thr[:, 2:3], ALU.mult, ALU.mult),
        waits=[("V", v_tc)])
    ps3_free = ("V", v_thr)

    f_order = []
    for hf in range(2):
        for eci in range(32):
            for which in ("u", "g"):
                for h2 in range(2):
                    f_order.append(("ug", hf, eci, which, h2))
        for dc in range(NDC):
            for sp4 in range(4):
                f_order.append(("d", hf, dc, sp4, 0))

    def f_src(item):
        if item[0] == "ug":
            _, hf, eci, which, h2 = item
            wh = wu_h if which == "u" else wg_h
            return wh[hf * 32 + eci][:, h2 * 1024:(h2 + 1) * 1024]
        _, hf, dc, sp4, _ = item
        off = hf * 4096 + sp4 * 1024
        return wd_h[dc][:, off:off + 1024]

    f_dy = {}
    f_sread = {}
    f_emitted = [0]

    def f_pump(upto):
        while f_emitted[0] <= upto and f_emitted[0] < len(f_order):
            n = f_emitted[0]
            waits = []
            if n < 2:
                waits.append(("V", p1_last_v))
            if n >= 2:
                waits.append(("S", f_sread[n - 2]))
            f_dy[n] = dma("y", "dy", fslot(n), f_src(f_order[n]), waits=waits)
            f_emitted[0] += 1

    fidx = [0]
    qtmp_free = {0: prev_ao}

    def quant_tile(dst, qslot_i, mat, s_extra=(), v_extra=()):
        n = fidx[0]
        fidx[0] += 1
        f_pump(n)
        qt = qtmp[:, qslot_i * 1024:(qslot_i + 1) * 1024]
        w1 = [("dy", f_dy[n]), ("V", v_thr)] + list(s_extra)
        if qtmp_free[qslot_i] is not None:
            w1.append(("V", qtmp_free[qslot_i]))
        act(qt, fslot(n), AF.Sign, bias=thr[:, 3 + mat:4 + mat], waits=w1)
        s2 = act(dst, fslot(n), AF.Sign, bias=thp[:, mat:mat + 1])
        f_sread[n] = s2
        f_pump(n + 2)
        va = vop(lambda e, d=dst, q=qt: e.tensor_tensor(d, d, q, ALU.add),
                 waits=[("S", s2)] + list(v_extra))
        qtmp_free[qslot_i] = va
        return va

    lastT = {}
    v_hid = {}
    t_hid_free = {}
    for hf in range(2):
        for eci in range(32):
            ec = hf * 32 + eci
            v_qu = None
            for h2 in range(2):
                sx = []
                if ec == 0 and h2 == 0:
                    sx = [("T", t_attn_last)]
                elif ("u", ec - 2) in lastT:
                    sx = [("T", lastT[("u", ec - 2)])]
                v_qu = quant_tile(
                    qbuf[:, (ec % 2) * 2048 + h2 * 1024:
                         (ec % 2) * 2048 + (h2 + 1) * 1024], 0, 0, s_extra=sx)
            v_qg = None
            for h2 in range(2):
                sx = []
                if ec == 0 and h2 == 0:
                    sx = [("T", t_d_last)]
                elif ("g", ec - 2) in lastT:
                    sx = [("T", lastT[("g", ec - 2)])]
                v_qg = quant_tile(
                    aog[:, (ec % 2) * 2048 + h2 * 1024:
                        (ec % 2) * 2048 + (h2 + 1) * 1024], 0, 1,
                    s_extra=sx)
            for th in range(2):
                for dc in range(NDC):
                    w = []
                    if th == 0 and dc == 0:
                        w = [("V", v_qu), ("V", v_ln2_all)]
                        if ps0_free is not None:
                            w.append(ps0_free)
                    t_up = mm(PS0[:, th * 512:(th + 1) * 512],
                              qbuf[:, (ec % 2) * 2048 + dc * 128:
                                   (ec % 2) * 2048 + (dc + 1) * 128],
                              lnx1(dc)[:, th * 512:(th + 1) * 512],
                              start=(dc == 0), stop=(dc == NDC - 1), waits=w,
                              inc=(th == 1 and dc == NDC - 1))
            lastT[("u", ec)] = t_up
            for th in range(2):
                for dc in range(NDC):
                    w = []
                    if th == 0 and dc == 0:
                        w = [("V", v_qg)]
                        if ps1_free is not None:
                            w.append(ps1_free)
                    t_gt = mm(PS1[:, th * 512:(th + 1) * 512],
                              aog[:, (ec % 2) * 2048 + dc * 128:
                                  (ec % 2) * 2048 + (dc + 1) * 128],
                              lnx1(dc)[:, th * 512:(th + 1) * 512],
                              start=(dc == 0), stop=(dc == NDC - 1), waits=w,
                              inc=(th == 1 and dc == NDC - 1))
            lastT[("g", ec)] = t_gt
            ws = [("T", t_up)]
            if ec >= 2:
                ws.append(("V", v_hid[ec - 2]))
            elif ec == 1:
                pass
            s_silu = act(hs_slot(eci), PS0[:, :], AF.Silu,
                         scale=thr[:, 6:7], waits=ws)
            ps0_free = ("S", s_silu)
            wh = [("T", t_gt), ("S", s_silu)]
            if (hf, eci) in t_hid_free:
                wh.append(("T", t_hid_free[(hf, eci)]))
            v_hid[ec] = vop(lambda e, eci=eci: e.tensor_tensor(
                hid(eci), PS1[:, :], hs_slot(eci), ALU.mult), waits=wh)
            ps1_free = ("V", v_hid[ec])
        v_hid_all = max(v_hid[hf * 32 + i] for i in range(32))
        dreg = [(PS2, 0, "a"), (PS2, 512, "b"), (PS3, 0, "c")]
        dreg_free = {
            "a": ("V", s_evac[("ao", NG - 1, HPG - 1, 0)]) if hf == 0 else dreg_free_sv["a"],
            "b": ("V", s_evac[("ao", NG - 1, HPG - 1, 1)]) if hf == 0 else dreg_free_sv["b"],
            "c": ps3_free if hf == 0 else dreg_free_sv["c"],
        }
        t_dn = None
        for dc in range(NDC):
            vqd = {}
            for jp in range(2):
                v_last_q = None
                for h2 in range(2):
                    sx = []
                    if ("d", hf, dc - 1) in lastT:
                        sx = [("T", lastT[("d", hf, dc - 1)])]
                    elif dc == 0 and jp == 0 and h2 == 0 and hf == 1:
                        sx = [("T", lastT[("d", 0, NDC - 1)])]
                    v_last_q = quant_tile(
                        qdb[:, (jp % 2) * 2048 + h2 * 1024:
                            (jp % 2) * 2048 + (h2 + 1) * 1024], 0, 2,
                        s_extra=sx)
                vqd[jp] = v_last_q
            for th in range(2):
                pst, off, rname = dreg[(dc * 2 + th) % 3]
                for jp in range(2):
                    for e16 in range(16):
                        eci = jp * 16 + e16
                        w = []
                        if e16 == 0:
                            w = [("V", vqd[jp])]
                            if jp == 0:
                                w.append(("V", v_hid_all))
                                if dreg_free[rname] is not None:
                                    w.append(dreg_free[rname])
                        t_dn = mm(pst[:, off:off + 512],
                                  qdb[:, (jp % 2) * 2048 + e16 * 128:
                                      (jp % 2) * 2048 + (e16 + 1) * 128],
                                  hid(eci)[:, th * 512:(th + 1) * 512],
                                  start=(eci == 0), stop=(eci == 31), waits=w,
                                  inc=(eci == 31))
                wv = [("T", t_dn), ("V", v_x1[dc])]
                v_x1[dc] = vop(
                    lambda e, dc=dc, pst=pst, off=off, th=th:
                    e.scalar_tensor_tensor(
                        x1v(dc)[:, th * 512:(th + 1) * 512],
                        pst[:, off:off + 512], thr[:, 7:8],
                        x1v(dc)[:, th * 512:(th + 1) * 512],
                        ALU.mult, ALU.add), waits=wv)
                dreg_free[rname] = ("V", v_x1[dc])
            lastT[("d", hf, dc)] = t_dn
        dreg_free_sv = dict(dreg_free)
        for i in range(32):
            t_hid_free[(hf + 1, i)] = t_dn

    v_last = max(v_x1.values())
    p.emit("s",
           lambda e: e.dma_start(
               out=out_h.rearrange("(c p) t -> p c t", p=128),
               in_=x1[:].rearrange("p (c t) -> p c t", c=NDC)),
           [("V", v_last)], [("da", 16)])
    p.emit("s", lambda e: e.wait_ge(sems["da"], 16), (), ())

    def replay(eng_obj, stream):
        lastw = {}
        for fn, waits, incs in stream:
            for sname, val in waits:
                if lastw.get(sname, -1) >= val:
                    continue
                eng_obj.wait_ge(sems[sname], val)
                lastw[sname] = val
            inst = fn(eng_obj)
            for sname, nv in incs:
                inst.then_inc(sems[sname], nv)

    with nc.Block() as block:
        @block.tensor
        def _(e):
            replay(e, p.streams["t"])

        @block.scalar
        def _(e):
            replay(e, p.streams["s"])

        @block.vector
        def _(e):
            replay(e, p.streams["v"])

        @block.gpsimd
        def _(e):
            replay(e, p.streams["g"])

        @block.sync
        def _(e):
            replay(e, p.streams["y"])

    es.close()
    return nc


_CACHE = {}


def _prep_weights(inputs):
    Wq = np.asarray(inputs["L0_attn_Wq"], np.float32)
    Wk = np.asarray(inputs["L0_attn_Wk"], np.float32)
    Wv = np.asarray(inputs["L0_attn_Wv"], np.float32)
    Wo = np.asarray(inputs["L0_attn_Wo"], np.float32)
    Wu = np.asarray(inputs["L0_ffn_W_up"], np.float32)
    Wg = np.asarray(inputs["L0_ffn_W_gate"], np.float32)
    Wd = np.asarray(inputs["L0_ffn_W_down"], np.float32)

    def qkv_slabs(W):
        return np.ascontiguousarray(
            W.reshape(16, 128, 16, 128).transpose(0, 3, 2, 1).reshape(16, 128, 2048))

    sq, sk, sv = qkv_slabs(Wq), qkv_slabs(Wk), qkv_slabs(Wv)
    wqkv = np.empty((48, 128, 2048), np.float32)
    for g in range(4):
        for l in range(4):
            wqkv[g * 12 + l] = sq[g * 4 + l]
            wqkv[g * 12 + 4 + l] = sk[g * 4 + l]
            wqkv[g * 12 + 8 + l] = sv[g * 4 + l]
    wo = qkv_slabs(Wo)
    wu = np.ascontiguousarray(
        Wu.reshape(64, 128, 16, 128).transpose(0, 3, 2, 1).reshape(64, 128, 2048))
    wg = np.ascontiguousarray(
        Wg.reshape(64, 128, 16, 128).transpose(0, 3, 2, 1).reshape(64, 128, 2048))
    wd = np.ascontiguousarray(
        Wd.reshape(16, 128, 64, 128).transpose(0, 3, 2, 1).reshape(16, 128, 8192))

    def col16(v):
        return np.ascontiguousarray(np.asarray(v, np.float32).reshape(16, 128).T)

    return dict(
        wqkv=wqkv, wo=wo, wu=wu, wg=wg, wd=wd,
        bq=col16(inputs["L0_attn_bq"]), bk=col16(inputs["L0_attn_bk"]),
        bv=col16(inputs["L0_attn_bv"]), bo=col16(inputs["L0_attn_bo"]),
        rot=col16(inputs["L0_attn_helix_rotation"]),
        spl=np.ascontiguousarray(np.tile(
            np.asarray(inputs["L0_attn_spiral_weight"], np.float32)[None, :],
            (128, 1))),
        identh=np.eye(128, dtype=md.bfloat16),
    )


def kernel(**inputs):
    if "nc" not in _CACHE:
        _CACHE["nc"] = build_nc()
    nc = _CACHE["nc"]

    x = np.asarray(inputs["L0_attn_x"], np.float32)
    com = _prep_weights(inputs)

    in_maps = []
    for c in range(8):
        b, h2 = c // 2, c % 2
        xbT = x[b].T
        if h2 == 0:
            xc = np.ascontiguousarray(xbT)
        else:
            xc = np.ascontiguousarray(
                np.concatenate([xbT[:, 1024:], xbT[:, :1024]], axis=1))
        m = dict(com)
        m["x"] = xc
        in_maps.append(m)

    res = run_bass_kernel_spmd(nc, in_maps, core_ids=list(range(8)))

    out = np.empty((4, 2048, 2048), np.float32)
    for c in range(8):
        b, h2 = c // 2, c % 2
        out[b, h2 * 1024:(h2 + 1) * 1024, :] = res.results[c]["out"].T
    return out
